# version 07
# baseline (speedup 1.0000x reference)
import numpy as np
import concourse.bass as bass
import concourse.mybir as mybir
from concourse.bass_utils import run_bass_kernel_spmd
from concourse.library_config import mlp as mlp_lib

F32 = mybir.dt.float32
BF16 = mybir.dt.bfloat16
I16 = mybir.dt.int16
AX = mybir.AxisListType.X
OP = mybir.AluOpType
AF = mybir.ActivationFunctionType

S = 4096
D = 2048
T = 2048
NH = 8
KD = D // 128
TCH = T // 128
F = 1408
FCH = F // 128
FS = 2816
CAP = 768
NCC = CAP // 128
NEXP = 8
NE_ALL = 10
SBIAS = -16.0
RBIAS = -8.0
EPS = 1e-5
KB = 1024
SUBT = [(0, 1536), (1536, 1536), (3072, 1024)]
STAGE = "full"


class Ctr:
    def __init__(self, name):
        self.name = name
        self.v = 0
        self.sem = None

    def bump(self, n=1):
        self.v += n
        return self.v


def build_program():
    nc = bass.Bass("TRN2", target_bir_lowering=False)

    xb = nc.dram_tensor("xb", [S, D], F32, kind="ExternalInput").ap()
    xown = nc.dram_tensor("xown", [T, D], F32, kind="ExternalInput").ap()
    wqk = nc.dram_tensor("wqk", [D, 2048], F32, kind="ExternalInput").ap()
    wv = nc.dram_tensor("wv", [D, D], F32, kind="ExternalInput").ap()
    wo = nc.dram_tensor("wo", [D, D], F32, kind="ExternalInput").ap()
    gwT = nc.dram_tensor("gwT", [D, NEXP], F32, kind="ExternalInput").ap()
    rwgT = nc.dram_tensor("rwgT", [NEXP, D, F], F32, kind="ExternalInput").ap()
    rwuT = nc.dram_tensor("rwuT", [NEXP, D, F], F32, kind="ExternalInput").ap()
    rwdT = nc.dram_tensor("rwdT", [NEXP, F, D], F32, kind="ExternalInput").ap()
    swgT = nc.dram_tensor("swgT", [D, FS], F32, kind="ExternalInput").ap()
    swuT = nc.dram_tensor("swuT", [D, FS], F32, kind="ExternalInput").ap()
    swdT = nc.dram_tensor("swdT", [FS, D], F32, kind="ExternalInput").ap()
    identb = nc.dram_tensor("identb", [128, 128], BF16,
                            kind="ExternalInput").ap()
    identf = nc.dram_tensor("identf", [16, 16], F32,
                            kind="ExternalInput").ap()
    hsv = nc.dram_tensor("hsv", [128, 2], F32, kind="ExternalInput").ap()
    cvec = nc.dram_tensor("cvec", [128, 4], F32, kind="ExternalInput").ap()
    lt128 = nc.dram_tensor("lt128", [128, 128], BF16,
                           kind="ExternalInput").ap()
    tixh = nc.dram_tensor("tixh", [128, 16], F32, kind="ExternalInput").ap()
    z16 = nc.dram_tensor("z16", [128, 16], F32, kind="ExternalInput").ap()
    cidx = nc.dram_tensor("cidx", [128, 128], I16, kind="ExternalInput").ap()
    out = nc.dram_tensor("out", [T, D], F32, kind="ExternalOutput").ap()
    dbg_kind = "ExternalOutput" if STAGE != "full" else "Internal"
    qsp = nc.dram_tensor("qsp", [NH, 128, S], BF16, kind=dbg_kind).ap()
    cc_in = nc.dram_tensor("cc_in", [NH, S], F32).ap()
    cc_out = nc.dram_tensor("cc_out", [2 * NH, S], F32).ap()
    h2nd = nc.dram_tensor("h2nd", [T, D], BF16).ap()
    idxw = [nc.dram_tensor(f"idxw{e}", [3072, 2], F32).ap()
            for e in range(NEXP)]
    dbg_cc = (nc.dram_tensor("dbg_cc", [3 * NH, S], F32,
                             kind="ExternalOutput").ap()
              if STAGE != "full" else None)
    dbg_ln = (nc.dram_tensor("dbg_ln", [128, KD * T], BF16,
                             kind="ExternalOutput").ap()
              if STAGE != "full" else None)

    ARENA = 204 * KB + 3776
    BASE = nc.bump_sbuf(ARENA)[0]
    BASE = ((BASE + 31) // 32) * 32

    def sb(name, shape, dtype, off):
        return nc.alloc_sbuf_tensor_at(name, shape, dtype,
                                       offset=BASE + off).ap()

    A0, B0, M0 = 0, 64 * KB, 128 * KB
    k8 = sb("k8", [128, NH * S], BF16, A0)
    wvsb = sb("wvsb", [128, KD * D], BF16, A0)
    wosb = sb("wosb", [128, KD * D], BF16, A0)
    gusb = sb("gusb", [128, FCH * T], BF16, A0)
    gusel = [sb("gusel0", [128, FCH * 1024], BF16, A0),
             sb("gusel1", [128, FCH * 1024], BF16, A0 + 22 * KB)]
    wgm = sb("wgm", [128, 2 * KD * 128], BF16, A0 + 44 * KB)
    wum = sb("wum", [128, 2 * KD * 128], BF16, A0 + 52 * KB)
    lt128sb = sb("lt128sb", [128, 128], BF16, A0 + 60 * KB)
    qidxf = sb("qidxf", [128, 128], F32, A0 + 60 * KB + 256)
    qidx16 = [sb("qidx16a", [128, 64], I16, A0 + 60 * KB + 768),
              sb("qidx16b", [128, 64], I16, A0 + 60 * KB + 896)]
    cidx16 = sb("cidx16", [128, 128], I16, A0 + 61 * KB)

    lnxT = sb("lnxT", [128, KD * T], BF16, B0)
    qh = sb("qh", [128, 2 * S], BF16, B0)
    ebuf = sb("ebuf", [128, 2 * S], BF16, B0 + 16 * KB)
    cvall = sb("cvall", [128, TCH * D], BF16, B0)
    h2nT = sb("h2nT", [128, KD * T], BF16, B0)
    h2selT = [sb("h2selT0", [128, KD * 1024], BF16, B0),
              sb("h2selT1", [128, KD * 1024], BF16, B0 + 32 * KB)]

    xc = sb("xc", [128, 2 * D], F32, M0)
    lnc = sb("lnc", [128, 2 * D], BF16, M0 + 16 * KB)
    wqm = sb("wqm", [128, 2 * KD * 128], BF16, M0 + 24 * KB)
    qstage = sb("qstage", [128, 4 * 512], BF16, M0 + 32 * KB)
    cs_stage = sb("cs_stage", [128, 2 * 1024], F32, M0 + 36 * KB)
    cc16f = sb("cc16f", [16, S], F32, M0 + 24 * KB)
    lnT = sb("lnT", [128, 2 * D], BF16, M0 + 52 * KB)
    cvT = sb("cvT", [128, 2 * D], BF16, M0 + 52 * KB)
    x2c = sb("x2c", [128, 2 * D], F32, M0 + 60 * KB)
    wdt = sb("wdt", [128, FCH * D], BF16, M0)
    gsil = sb("gsil", [128, 2 * T], BF16, M0 + 44 * KB)
    gstg = [sb("gstg0", [128, D], BF16, M0 + 52 * KB),
            sb("gstg1", [128, D], BF16, M0 + 56 * KB)]
    eptmp = sb("eptmp", [128, 2 * D], F32, M0 + 60 * KB)

    P0 = 204 * KB
    _tail = [P0]

    TOFF = {}

    def tl(name, shape, dtype):
        off = _tail[0]
        TOFF[name] = off
        nbytes = shape[1] * (2 if dtype == BF16 else 4)
        _tail[0] = off + ((nbytes + 31) // 32) * 32
        assert _tail[0] <= ARENA, name
        return sb(name, shape, dtype, off)

    idb = tl("idb", [128, 128], BF16)
    idf = tl("idf", [16, 16], F32)
    gwsb = tl("gwsb", [128, KD * NEXP], BF16)
    scvec = tl("scvec", [128, 256], F32)
    wmat = tl("wmat", [128, TCH * NEXP], F32)
    hssb = tl("hssb", [128, 2], F32)
    bnout = tl("bnout", [128, 2 * 24], F32)
    aggr = tl("aggr", [128, 2 * 2], F32)
    sstd = tl("sstd", [128, 2], F32)
    rstd = tl("rstd", [128, 2], F32)
    nbias = tl("nbias", [128, 2], F32)
    zsum = tl("zsum", [128, 6], F32)
    zs = tl("zs", [128, 1], F32)
    zr = tl("zr", [128, 1], F32)
    rbf = tl("rbf", [128, 2], BF16)
    pr8 = tl("pr8", [128, 8], F32)
    sc8 = tl("sc8", [128, 8], F32)
    gs4 = tl("gs4", [128, 4], F32)
    rk4 = tl("rk4", [128, 4], F32)
    gm4 = tl("gm4", [128, 4], F32)
    msk8 = tl("msk8", [128, 8], F32)
    rk8 = tl("rk8", [128, 8], F32)
    sel8 = tl("sel8", [128, 8], F32)
    t8 = tl("t8", [128, 8], F32)
    zrr = tl("zrr", [128, 1], F32)
    csb = tl("csb", [128, 4], F32)
    pr8all = tl("pr8all", [128, TCH * NEXP], F32)
    zrall = tl("zrall", [128, TCH], F32)
    rmx = tl("rmx", [128, 4], F32)

    _sp1 = TOFF["pr8"]
    pfxA = sb("pfxA", [128, 16], F32, _sp1)
    pfxB = sb("pfxB", [128, 16], F32, _sp1 + 64)
    insd = sb("insd", [128, 16], F32, _sp1 + 128)
    rs8 = sb("rs8", [128, 8], F32, _sp1 + 192)
    rs8bf = sb("rs8bf", [128, 8], BF16, _sp1 + 224)
    tixsb = sb("tixsb", [128, 16], F32, _sp1 + 256)
    _sp2 = TOFF["pr8all"]
    offs = sb("offs", [128, 16], F32, _sp2)
    offi = sb("offi", [128, 16], mybir.dt.int32, _sp2 + 64)
    pay = sb("pay", [128, 32], F32, _sp2 + 128)
    idxld = [sb("idxld0", [128, 16], F32, _sp2 + 256),
             sb("idxld1", [128, 16], F32, _sp2 + 320)]
    idxsel = [sb("idxsel0", [128, 8], mybir.dt.int32, _sp2 + 384),
              sb("idxsel1", [128, 8], mybir.dt.int32, _sp2 + 416)]
    wsel = [sb("wsel0", [128, 8], F32, _sp2 + 448),
            sb("wsel1", [128, 8], F32, _sp2 + 480)]
    t1all = sb("t1all", [128, 8], F32, _sp2 + 512)

    ps_a = nc.alloc_psum_tensor("ps_a", [128, 2048], F32).ap()
    ps_b = nc.alloc_psum_tensor("ps_b", [128, 2048], F32).ap()

    lnxT3 = lnxT.rearrange("p (k r) -> p k r", k=KD)
    h2nT3 = h2nT.rearrange("p (k r) -> p k r", k=KD)
    psa3 = ps_a.rearrange("p (k r) -> p k r", k=KD)

    ctrs = {}

    def C(name):
        if name not in ctrs:
            ctrs[name] = Ctr(name)
        return ctrs[name]

    sched = {e: [] for e in ("tensor", "vector", "scalar", "sync", "gpsimd")}

    def em(eng, fn):
        sched[eng].append(fn)

    q7reg = {}

    def f_glib(g):
        g.load_library(mlp_lib)
        q7reg["n128"] = g.alloc_register("n128")
        g.reg_mov(q7reg["n128"], 128)
    em("gpsimd", f_glib)

    d_id = C("d_id")
    d_x = [C("d_x0"), C("d_x1")]
    d_wqk = [C("d_wqk0"), C("d_wqk1")]
    d_q = [C("d_q0"), C("d_q1"), C("d_q2"), C("d_q3")]
    c_stat = C("c_stat")
    c_bns = C("c_bns")
    c_vch = C("c_vch")
    c_sq = C("c_sq")
    c_rst = C("c_rst")
    c_ln = C("c_ln")
    c_tp = C("c_tp")
    c_lnT = C("c_lnT")
    c_qkmm = C("c_qkmm")
    c_qkev = C("c_qkev")
    d_qin = [C("d_qin0"), C("d_qin1")]
    c_sc = C("c_sc")
    c_exp = C("c_exp")
    c_r = C("c_r")
    c_mv = C("c_mv")
    c_csev = C("c_csev")
    d_ccin = [C("d_ccin0"), C("d_ccin1")]
    c_cc = C("c_cc")
    d_cc16 = C("d_cc16")
    c_sct = C("c_sct")
    c_scv = C("c_scv")
    d_wv = C("d_wv")
    d_wo = C("d_wo")
    c_vmm = C("c_vmm")
    c_cvev = C("c_cvev")
    c_omm = C("c_omm")
    c_add = C("c_add")
    d_o = [C("d_o0"), C("d_o1")]
    d_gw8 = C("d_gw8")
    c_rmm = C("c_rmm")
    c_rxp = C("c_rxp")
    c_rt = C("c_rt")
    d_wd = C("d_wd")
    d_wg = [C("d_wg0"), C("d_wg1")]
    d_wu = [C("d_wu0"), C("d_wu1")]
    c_gmm = C("c_gmm")
    c_umm = C("c_umm")
    c_gev = C("c_gev")
    c_uev = C("c_uev")
    c_dmm = C("c_dmm")
    c_ep = C("c_ep")
    d_acc = [C("d_acc0"), C("d_acc1")]
    c_idcp = C("c_idcp")

    d_id.bump(64)

    def f_id(sy):
        sy.dma_start(out=idb[:, :], in_=identb).then_inc(d_id.sem, 16)
        sy.dma_start(out=idf[:, :], in_=identf).then_inc(d_id.sem, 16)
        sy.dma_start(out=hssb[:, :], in_=hsv).then_inc(d_id.sem, 16)
        sy.dma_start(out=csb[:, :], in_=cvec).then_inc(d_id.sem, 16)
    em("sync", f_id)
    ID_DONE = 64

    def ln_chunk(gc, src_ap, dst_evac, sync_extra=(), tp_extra=(),
                 ev_extra=(), ap_extra=()):
        par = gc % 2
        ln_free = max(0, c_ln.v - 1)
        xin = d_x[par].bump(16)
        first = (gc == 0)

        def f_sync(sy, src=src_ap, par=par, lnf=ln_free, ex=tuple(sync_extra)):
            for ct, val in ex:
                sy.wait_ge(ct.sem, val)
            if lnf:
                sy.wait_ge(c_ln.sem, lnf)
            sy.dma_start(out=xc[:, par * D:(par + 1) * D],
                         in_=src).then_inc(d_x[par].sem, 16)
        em("sync", f_sync)

        st_v = c_stat.bump()
        bns_v = c_bns.bump()

        def f_vst(v, par=par, xin=xin, first=first, bv=bns_v):
            if first:
                v.wait_ge(d_id.sem, ID_DONE)
            v.wait_ge(d_x[par].sem, xin)
            for q in range(4):
                mm = v.bn_stats(bnout[:, 24 * par + 6 * q:24 * par + 6 * q + 6],
                                xc[:, par * D + 512 * q:par * D + 512 * (q + 1)])
            mm.then_inc(c_bns.sem, 1)
            v.wait_ge(c_bns.sem, bv)
            v.bn_aggr(aggr[:, 2 * par:2 * par + 2],
                      bnout[:, 24 * par:24 * par + 24]).then_inc(c_stat.sem, 1)
        em("vector", f_vst)

        sq_v = c_sq.bump()

        def f_ssq(s, sv=st_v, par=par):
            s.wait_ge(c_stat.sem, sv)
            s.activation(sstd[:, par:par + 1], aggr[:, 2 * par + 1:2 * par + 2],
                         AF.Sqrt, bias=csb[:, 0:1]).then_inc(c_sq.sem, 1)
        em("scalar", f_ssq)

        rs_v = c_rst.bump()

        vch1 = c_vch.bump()

        def f_vrs(v, sv=sq_v, v1=vch1, par=par):
            v.wait_ge(c_sq.sem, sv)
            v.reciprocal(rstd[:, par:par + 1],
                         sstd[:, par:par + 1]).then_inc(c_vch.sem, 1)
            v.wait_ge(c_vch.sem, v1)
            v.scalar_tensor_tensor(nbias[:, par:par + 1],
                                   aggr[:, 2 * par:2 * par + 1], -1.0,
                                   rstd[:, par:par + 1], OP.mult,
                                   OP.mult).then_inc(c_rst.sem, 1)
        em("vector", f_vrs)

        ln_v = c_ln.bump()

        def f_sap(s, par=par, rv=rs_v, ex=tuple(ap_extra)):
            s.wait_ge(c_rst.sem, rv)
            for ct, val in ex:
                s.wait_ge(ct.sem, val)
            s.activation(lnc[:, par * D:(par + 1) * D],
                         xc[:, par * D:(par + 1) * D], AF.Identity,
                         bias=nbias[:, par:par + 1],
                         scale=rstd[:, par:par + 1]).then_inc(c_ln.sem, 1)
        em("scalar", f_sap)

        tp_v = c_tp.bump()
        lnT_free = c_lnT.v

        def f_ttp(t, par=par, lv=ln_v, lf=lnT_free, ex=tuple(tp_extra)):
            t.wait_ge(c_ln.sem, lv)
            for ct, val in ex:
                t.wait_ge(ct.sem, val)
            if lf:
                t.wait_ge(c_lnT.sem, lf)
            for k in range(KD):
                mm = t.matmul(ps_a[:, 128 * k:128 * (k + 1)],
                              lnc[:, par * D + 128 * k:par * D + 128 * (k + 1)],
                              idb[:, :], start=True, stop=True)
            mm.then_inc(c_tp.sem, 1)
        em("tensor", f_ttp)

        c_lnT.bump()

        def f_vev(v, tv=tp_v, ex=tuple(ev_extra), de=dst_evac):
            v.wait_ge(c_tp.sem, tv)
            for ct, val in ex:
                v.wait_ge(ct.sem, val)
            de(v)
        em("vector", f_vev)

    n_qtile = 0
    for slab in range(2):
        for i in range(16):
            gc = 16 * slab + i

            def mk_dst(i=i):
                def de(v):
                    v.tensor_copy(lnxT3[:, :, 128 * i:128 * (i + 1)],
                                  psa3[:, :, :]).then_inc(c_lnT.sem, 1)
                return de

            ev_extra = []
            if slab == 1 and i == 0:
                ev_extra.append((c_qkmm, 64))
                if dbg_ln is not None:
                    ev_extra.append((d_id, ID_DONE + 16))
            ln_chunk(gc, xb[128 * gc:128 * (gc + 1), :], mk_dst(),
                     ev_extra=ev_extra)

        lnT_ready = c_lnT.v
        if dbg_ln is not None and slab == 0:
            d_id.bump(16)

            def f_sdbgln(sy, lr=lnT_ready):
                sy.wait_ge(c_lnT.sem, lr)
                sy.dma_start(out=dbg_ln, in_=lnxT[:, :]).then_inc(
                    d_id.sem, 16)
            em("sync", f_sdbgln)
        for m in range(16):
            sl = 16 * slab + m
            wq_v = d_wqk[sl % 2].bump(16)

            def f_gw(g, m=m, sl=sl):
                if sl >= 2:
                    g.wait_ge(c_qkmm.sem, (sl - 1) * 4)
                g.dma_start(
                    out=wqm[:, (sl % 2) * 2048:(sl % 2 + 1) * 2048].rearrange(
                        "p (k c) -> p k c", k=KD),
                    in_=wqk[:, 128 * m:128 * (m + 1)].rearrange(
                        "(k p) c -> p k c", p=128)).then_inc(d_wqk[sl % 2].sem,
                                                             16)
            em("gpsimd", f_gw)

            for nt in range(4):
                tile = 4 * sl + nt
                mm_v = c_qkmm.bump()
                pslo = (tile % 4) * 512

                def f_tqk(t, sl=sl, nt=nt, tile=tile, lo=pslo, wqv=wq_v,
                          lr=lnT_ready):
                    t.wait_ge(d_wqk[sl % 2].sem, wqv)
                    t.wait_ge(c_lnT.sem, lr)
                    if tile >= 4:
                        t.wait_ge(c_qkev.sem, tile - 3)
                    for k in range(KD):
                        mm = t.matmul(
                            ps_b[:, lo:lo + 512],
                            wqm[:, (sl % 2) * 2048 + 128 * k:
                                (sl % 2) * 2048 + 128 * (k + 1)],
                            lnxT3[:, k, nt * 512:(nt + 1) * 512],
                            start=(k == 0), stop=(k == KD - 1))
                    mm.then_inc(c_qkmm.sem, 1)
                em("tensor", f_tqk)

                ev_v = c_qkev.bump()
                if m < 8:
                    qi = n_qtile
                    n_qtile += 1

                    def f_sev(s, mmv=mm_v, lo=pslo, qi=qi):
                        s.wait_ge(c_qkmm.sem, mmv)
                        if qi >= 4:
                            s.wait_ge(d_q[qi % 4].sem, 16 * (qi // 4))
                        s.activation(qstage[:, (qi % 4) * 512:
                                            (qi % 4 + 1) * 512],
                                     ps_b[:, lo:lo + 512],
                                     AF.Copy).then_inc(c_qkev.sem, 1)
                    em("scalar", f_sev)
                    d_q[qi % 4].bump(16)

                    def f_syq(sy, ev=ev_v, m=m, nt=nt, slab=slab, qi=qi):
                        sy.wait_ge(c_qkev.sem, ev)
                        sy.dma_start(
                            out=qsp[m, :, slab * T + nt * 512:
                                    slab * T + (nt + 1) * 512],
                            in_=qstage[:, (qi % 4) * 512:(qi % 4 + 1) * 512]
                        ).then_inc(d_q[qi % 4].sem, 16)
                    em("scalar", f_syq)
                else:
                    def f_sev(s, mmv=mm_v, lo=pslo, m=m, nt=nt, slab=slab):
                        s.wait_ge(c_qkmm.sem, mmv)
                        s.activation(
                            k8[:, (m - 8) * S + slab * T + nt * 512:
                               (m - 8) * S + slab * T + (nt + 1) * 512],
                            ps_b[:, lo:lo + 512],
                            AF.Copy).then_inc(c_qkev.sem, 1)
                    em("scalar", f_sev)

    S1_QKMM = c_qkmm.v
    S1_QKEV = c_qkev.v
    S1_DQ = [s.v for s in d_q]

    def emit_matvec(hj):
        h, j = divmod(hj, 32)
        par = hj % 2
        c_mv.bump()

        def f_tmv(t, h=h, j=j, par=par, rv=hj + 1):
            t.wait_ge(c_r.sem, rv)
            if j == 0:
                t.wait_ge(c_csev.sem, 2 * h)
            for gk in range(8):
                dst = ps_a if gk < 4 else ps_b
                gg = gk % 4
                mm = t.matmul(dst[32 * gg:32 * gg + 1, 1536:2048],
                              rbf[:, par:par + 1],
                              ebuf[:, par * S + 512 * gk:
                                   par * S + 512 * (gk + 1)],
                              start=(j == 0), stop=(j == 31),
                              tile_position=(0, 32 * gg))
            mm.then_inc(c_mv.sem, 1)
        em("tensor", f_tmv)

    def emit_head_end(h):
        hp = h % 2
        c_csev.bump()
        ev2 = c_csev.bump()

        def f_vcs(v, h=h, hp=hp):
            v.wait_ge(c_mv.sem, 32 * (h + 1))
            if h >= 2:
                v.wait_ge(d_ccin[hp].sem, 32 * (h // 2))
            for gg in range(4):
                mmv = v.tensor_copy(
                    cs_stage[32 * gg:32 * gg + 1,
                             hp * 1024:hp * 1024 + 512],
                    ps_a[32 * gg:32 * gg + 1, 1536:2048])
            mmv.then_inc(c_csev.sem, 1)
            for gg in range(4):
                mmv = v.tensor_copy(
                    cs_stage[32 * gg:32 * gg + 1,
                             hp * 1024 + 512:(hp + 1) * 1024],
                    ps_b[32 * gg:32 * gg + 1, 1536:2048])
            mmv.then_inc(c_csev.sem, 1)
        em("vector", f_vcs)
        d_ccin[hp].bump(32)

        def f_sycs(sy, h=h, hp=hp, ev=ev2):
            sy.wait_ge(c_csev.sem, ev)
            sy.dma_start(out=cc_in[h, 0:2048],
                         in_=cs_stage[0:97:32, hp * 1024:hp * 1024 + 512]
                         ).then_inc(d_ccin[hp].sem, 16)
            sy.dma_start(out=cc_in[h, 2048:4096],
                         in_=cs_stage[0:97:32, hp * 1024 + 512:(hp + 1) * 1024]
                         ).then_inc(d_ccin[hp].sem, 16)
        em("gpsimd", f_sycs)

    for hj in range(256):
        h, j = divmod(hj, 32)
        par = hj % 2
        if j == 0:
            d_qin[h % 2].bump(16)

            def f_syq(sy, h=h):
                if h == 0:
                    sy.wait_ge(c_qkmm.sem, S1_QKMM)
                    for qq in range(4):
                        sy.wait_ge(d_q[qq].sem, S1_DQ[qq])
                if h >= 2:
                    sy.wait_ge(c_mv.sem, 32 * (h - 1))
                sy.dma_start(out=qh[:, (h % 2) * S:(h % 2 + 1) * S],
                             in_=qsp[h, :, :]).then_inc(d_qin[h % 2].sem, 16)
            em("gpsimd", f_syq)

        for st, (koff, kw) in enumerate(SUBT):
            su = 3 * hj + st
            reg = ps_a if su % 2 == 0 else ps_b
            sc_v = c_sc.bump()

            def f_tsc(t, h=h, j=j, su=su, koff=koff, kw=kw, reg=reg):
                if su == 0:
                    t.wait_ge(c_qkev.sem, S1_QKEV)
                t.wait_ge(d_qin[h % 2].sem, 16 * (h // 2 + 1))
                if su >= 2:
                    t.wait_ge(c_exp.sem, su - 1)
                for kk in range(kw // 512):
                    mm = t.matmul(
                        reg[:, 512 * kk:512 * (kk + 1)],
                        qh[:, (h % 2) * S + 128 * j:
                           (h % 2) * S + 128 * (j + 1)],
                        k8[:, h * S + koff + 512 * kk:
                           h * S + koff + 512 * (kk + 1)],
                        start=True, stop=True)
                mm.then_inc(c_sc.sem, 1)
            em("tensor", f_tsc)

            c_exp.bump()

            def f_sexp(s, st=st, koff=koff, kw=kw, reg=reg, par=par,
                       scv=sc_v, hj=hj):
                s.wait_ge(c_sc.sem, scv)
                if hj >= 2:
                    s.wait_ge(c_mv.sem, hj - 1)
                s.activation(ebuf[:, par * S + koff:par * S + koff + kw],
                             reg[:, 0:kw], AF.Exp, bias=csb[:, 1:2],
                             scale=0.08838834764831845,
                             accum_out=zsum[:, 3 * par + st:3 * par + st + 1]
                             ).then_inc(c_exp.sem, 1)
            em("scalar", f_sexp)

        c_r.bump()

        vch1 = c_vch.bump()
        vch2 = c_vch.bump()

        def f_vr(v, par=par, ev=c_exp.v, v1=vch1, v2=vch2):
            v.wait_ge(c_exp.sem, ev)
            v.tensor_reduce(zs[:, :], zsum[:, 3 * par:3 * par + 3], AX,
                            OP.add).then_inc(c_vch.sem, 1)
            v.wait_ge(c_vch.sem, v1)
            v.reciprocal(zr[:, :], zs[:, :]).then_inc(c_vch.sem, 1)
            v.wait_ge(c_vch.sem, v2)
            v.tensor_copy(rbf[:, par:par + 1], zr[:, :]).then_inc(c_r.sem, 1)
        em("vector", f_vr)

        if hj >= 1:
            emit_matvec(hj - 1)
            if j == 0 and hj >= 32:
                emit_head_end(h - 1)
    emit_matvec(255)
    emit_head_end(7)

    c_cc.bump()

    def f_gcc(g):
        g.wait_ge(d_ccin[0].sem, 128)
        g.wait_ge(d_ccin[1].sem, 128)
        g.collective_compute(
            "AllGather", OP.bypass,
            replica_groups=[[0, 1], [2, 3], [4, 5], [6, 7]],
            ins=[cc_in], outs=[cc_out]).then_inc(c_cc.sem, 1)
    em("gpsimd", f_gcc)

    d_cc16.bump(16)

    def f_sy16(sy):
        sy.wait_ge(c_cc.sem, 1)
        sy.dma_start(out=cc16f[:, :], in_=cc_out).then_inc(d_cc16.sem, 16)
    em("gpsimd", f_sy16)
    if dbg_cc is not None:
        d_cc16.bump(32)

        def f_sydbg(sy):
            sy.wait_ge(d_cc16.sem, 16)
            sy.dma_start(out=dbg_cc[0:NH, :], in_=cc_in).then_inc(
                d_cc16.sem, 16)
            sy.dma_start(out=dbg_cc[NH:3 * NH, :], in_=cc_out).then_inc(
                d_cc16.sem, 16)
            sy.wait_ge(d_cc16.sem, 48)
        em("gpsimd", f_sydbg)
    c_sct.bump()

    def f_tsct(t):
        t.wait_ge(d_cc16.sem, 16)
        t.wait_ge(c_csev.sem, 16)
        for g32 in range(32):
            mm = t.matmul(ps_a[:, 16 * g32:16 * (g32 + 1)],
                          cc16f[:, 128 * g32:128 * (g32 + 1)],
                          idf[:, :], start=True, stop=True)
        mm.then_inc(c_sct.sem, 1)
    em("tensor", f_tsct)
    c_scv.bump()

    vblend = c_vch.bump()

    def f_vscv(v, vb=vblend):
        v.wait_ge(c_sct.sem, 1)
        v.tensor_scalar(scvec[:, :], ps_a[:, 0:256], hssb[:, 0:1], None,
                        OP.mult).then_inc(c_vch.sem, 1)
        v.wait_ge(c_vch.sem, vb)
        v.scalar_tensor_tensor(scvec[:, :], ps_a[:, 256:512],
                               hssb[:, 1:2], scvec[:, :], OP.mult,
                               OP.add).then_inc(c_scv.sem, 1)
    em("vector", f_vscv)

    for k in range(KD):
        d_wv.bump(16)

        def f_gwv(g, k=k):
            if k == 0:
                g.wait_ge(c_mv.sem, 256)
            g.dma_start(out=wvsb[:, k * D:(k + 1) * D],
                        in_=wv[128 * k:128 * (k + 1), :]).then_inc(d_wv.sem, 16)
        em("gpsimd", f_gwv)
    WV_DONE = d_wv.v

    for tc in range(TCH):
        gc = 32 + tc
        par = gc % 2

        def mk_dst(par=par):
            def de(v):
                v.tensor_copy(
                    lnT[:, par * D:(par + 1) * D].rearrange(
                        "p (k r) -> p k r", k=KD),
                    psa3[:, :, :]).then_inc(c_lnT.sem, 1)
            return de

        tp_extra = []
        ev_extra = []
        if tc == 0:
            tp_extra = [(c_scv, 1)]
        if tc >= 2:
            ev_extra = [(c_vmm, 4 * (tc - 1))]
        ln_chunk(gc, xown[128 * tc:128 * (tc + 1), :], mk_dst(),
                 tp_extra=tp_extra, ev_extra=ev_extra)

        lnT_v = c_lnT.v
        for nt in range(4):
            mm_v = c_vmm.bump()

            def f_tv(t, tc=tc, nt=nt, par=par, lv=lnT_v, first=(nt == 0)):
                if first:
                    t.wait_ge(d_wv.sem, WV_DONE)
                t.wait_ge(c_lnT.sem, lv)
                if tc >= 1 and nt == 0:
                    t.wait_ge(c_cvev.sem, 16 * tc)
                for k in range(KD):
                    mm = t.matmul(
                        ps_b[:, nt * 512:(nt + 1) * 512],
                        lnT[:, par * D + 128 * k:par * D + 128 * (k + 1)],
                        wvsb[:, k * D + nt * 512:k * D + (nt + 1) * 512],
                        start=(k == 0), stop=(k == KD - 1))
                mm.then_inc(c_vmm.sem, 1)
            em("tensor", f_tv)

        for h16 in range(16):
            c_cvev.bump()

            def f_scv(s, tc=tc, h16=h16, need=4 * tc + h16 // 4 + 1):
                s.wait_ge(c_vmm.sem, need)
                if tc == 0 and h16 == 0:
                    s.wait_ge(c_scv.sem, 1)
                s.activation(cvall[:, tc * D + 128 * h16:
                                   tc * D + 128 * (h16 + 1)],
                             ps_b[:, 128 * h16:128 * (h16 + 1)], AF.Copy,
                             scale=scvec[:, tc * 16 + h16:tc * 16 + h16 + 1]
                             ).then_inc(c_cvev.sem, 1)
            em("scalar", f_scv)
    S3A_VMM = c_vmm.v
    S3A_CVEV = c_cvev.v

    for k in range(KD):
        d_wo.bump(16)

        def f_gwo(g, k=k):
            if k == 0:
                g.wait_ge(c_vmm.sem, S3A_VMM)
            g.dma_start(out=wosb[:, k * D:(k + 1) * D],
                        in_=wo[128 * k:128 * (k + 1), :]).then_inc(d_wo.sem, 16)
        em("gpsimd", f_gwo)
    WO_DONE = d_wo.v

    c_cvT = C("c_cvT")
    c_cvTe = C("c_cvTe")
    for tc in range(TCH):
        par = tc % 2
        tp_v = c_cvT.bump()

        def f_tcvT(t, tc=tc, par=par, cvn=S3A_CVEV if tc == 0 else 0,
                   ef=c_cvTe.v):
            if tc == 0:
                t.wait_ge(c_cvev.sem, S3A_CVEV)
                t.wait_ge(c_lnT.sem, 48)
            if ef:
                t.wait_ge(c_cvTe.sem, ef)
            for k in range(KD):
                mm = t.matmul(ps_a[:, 128 * k:128 * (k + 1)],
                              cvall[:, tc * D + 128 * k:tc * D + 128 * (k + 1)],
                              idb[:, :], start=True, stop=True)
            mm.then_inc(c_cvT.sem, 1)
        em("tensor", f_tcvT)

        c_cvTe.bump()

        def f_vcvT(v, tv=tp_v, par=par, tc=tc):
            v.wait_ge(c_cvT.sem, tv)
            if tc >= 2:
                v.wait_ge(c_omm.sem, 4 * (tc - 1))
            v.tensor_copy(cvT[:, par * D:(par + 1) * D].rearrange(
                "p (k r) -> p k r", k=KD), psa3[:, :, :]
            ).then_inc(c_cvTe.sem, 1)
        em("vector", f_vcvT)

        d_x[par].bump(16)
        xin2 = d_x[par].v

        def f_syx(sy, tc=tc, par=par):
            if tc < 2:
                sy.wait_ge(c_ln.sem, 47 + tc)
            else:
                sy.wait_ge(c_add.sem, tc - 1)
            sy.dma_start(out=xc[:, par * D:(par + 1) * D],
                         in_=xown[128 * tc:128 * (tc + 1), :]
                         ).then_inc(d_x[par].sem, 16)
        em("sync", f_syx)

        ev_v = c_cvTe.v
        for nt in range(4):
            c_omm.bump()

            def f_to(t, tc=tc, nt=nt, par=par, ev=ev_v):
                if tc == 0 and nt == 0:
                    t.wait_ge(d_wo.sem, WO_DONE)
                t.wait_ge(c_cvTe.sem, ev)
                if nt == 0 and tc >= 1:
                    t.wait_ge(c_add.sem, tc)
                for k in range(KD):
                    mm = t.matmul(
                        ps_b[:, nt * 512:(nt + 1) * 512],
                        cvT[:, par * D + 128 * k:par * D + 128 * (k + 1)],
                        wosb[:, k * D + nt * 512:k * D + (nt + 1) * 512],
                        start=(k == 0), stop=(k == KD - 1))
                mm.then_inc(c_omm.sem, 1)
            em("tensor", f_to)

        c_add.bump()

        def f_vadd(v, tc=tc, par=par, on=c_omm.v, xin=xin2):
            v.wait_ge(c_omm.sem, on)
            v.wait_ge(d_x[par].sem, xin)
            if tc >= 2:
                v.wait_ge(d_o[par].sem, 16 * (tc // 2))
            v.tensor_tensor(x2c[:, par * D:(par + 1) * D],
                            ps_b[:, :], xc[:, par * D:(par + 1) * D],
                            OP.add).then_inc(c_add.sem, 1)
        em("vector", f_vadd)

        d_o[par].bump(16)

        def f_syo(sy, tc=tc, par=par, av=c_add.v):
            sy.wait_ge(c_add.sem, av)
            sy.dma_start(out=out[128 * tc:128 * (tc + 1), :],
                         in_=x2c[:, par * D:(par + 1) * D]
                         ).then_inc(d_o[par].sem, 16)
        em("sync", f_syo)
    S3B_OMM = c_omm.v
    S3B_DO = [s.v for s in d_o]

    d_gw8.bump(16)

    def f_ggw(g):
        g.wait_ge(c_omm.sem, S3B_OMM)
        g.dma_start(out=gwsb[:, :].rearrange("p (k c) -> p k c", k=KD),
                    in_=gwT.rearrange("(k p) c -> p k c", p=128)
                    ).then_inc(d_gw8.sem, 16)
    em("gpsimd", f_ggw)

    d_h2n = [C("d_h2n0"), C("d_h2n1")]
    for tc in range(TCH):
        gc = 48 + tc
        par = gc % 2

        def mk_dst(tc=tc):
            def de(v):
                v.tensor_copy(h2nT3[:, :, 128 * tc:128 * (tc + 1)],
                              psa3[:, :, :]).then_inc(c_lnT.sem, 1)
            return de

        sync_extra = [(d_o[tc % 2], 16 * (tc // 2 + 1))]
        if tc < 2:
            sync_extra.append((c_add, 16))
        tp_extra = []
        ev_extra = []
        ap_extra = []
        if tc == 0:
            tp_extra = [(c_cvTe, 16)]
            ev_extra = [(c_omm, S3B_OMM)]
        if tc >= 2:
            ap_extra = [(d_h2n[tc % 2], 16 * (tc // 2))]
        ln_chunk(gc, out[128 * tc:128 * (tc + 1), :], mk_dst(),
                 sync_extra=sync_extra, tp_extra=tp_extra, ev_extra=ev_extra,
                 ap_extra=ap_extra)

        d_h2n[par].bump(16)

        def f_sh2n(s, tc=tc, par=par, lv=c_ln.v):
            s.wait_ge(c_ln.sem, lv)
            s.dma_start(out=h2nd[128 * tc:128 * (tc + 1), :],
                        in_=lnc[:, par * D:(par + 1) * D]
                        ).then_inc(d_h2n[par].sem, 16)
        em("scalar", f_sh2n)

        h2_v = c_lnT.v
        c_rmm.bump()

        def f_trt(t, tc=tc, hv=h2_v):
            t.wait_ge(c_lnT.sem, hv)
            t.wait_ge(d_gw8.sem, 16)
            if tc == 0:
                t.wait_ge(c_add.sem, 16)
            if tc >= 1:
                t.wait_ge(c_rxp.sem, tc)
            for k in range(KD):
                mm = t.matmul(ps_b[:, 0:8],
                              h2nT3[:, k, 128 * tc:128 * (tc + 1)],
                              gwsb[:, 8 * k:8 * (k + 1)],
                              start=(k == 0), stop=(k == KD - 1))
            mm.then_inc(c_rmm.sem, 1)
        em("tensor", f_trt)

        c_rxp.bump()

        def f_srt(s, tc=tc, rv=c_rmm.v):
            s.wait_ge(c_rmm.sem, rv)
            s.activation(pr8all[:, 8 * tc:8 * (tc + 1)], ps_b[:, 0:8],
                         AF.Exp, bias=csb[:, 2:3],
                         accum_out=zrall[:, tc:tc + 1]).then_inc(c_rxp.sem, 1)
        em("scalar", f_srt)

    for tc in range(TCH):
        NSTEP = 15
        vbase = c_vch.v
        c_vch.bump(NSTEP)
        c_rt.bump()

        def f_vrt(v, tc=tc, xv=tc + 1, vb=vbase):
            pos = [vb]

            def ch(inst):
                pos[0] += 1
                inst.then_inc(c_vch.sem, 1)
                v.wait_ge(c_vch.sem, pos[0])
            pr = pr8all[:, 8 * tc:8 * (tc + 1)]
            v.wait_ge(c_rxp.sem, xv)
            ch(v.reciprocal(zs[:, :], zrall[:, tc:tc + 1]))
            ch(v.tensor_scalar(sc8[:, :], pr, zs[:, :], None, OP.mult))
            ch(v.tensor_tensor(gs4[:, :], sc8[:, 0:8:2], sc8[:, 1:8:2],
                               OP.max))
            ch(v.tensor_reduce(rmx[:, 0:1], gs4[:, :], AX, OP.max))
            ch(v.tensor_scalar(t8[:, 0:4], gs4[:, :], rmx[:, 0:1], None,
                               OP.is_equal))
            ch(v.scalar_tensor_tensor(rk4[:, :], t8[:, 0:4], -1e30,
                                      gs4[:, :], OP.mult, OP.add))
            ch(v.tensor_reduce(rmx[:, 1:2], rk4[:, :], AX, OP.max))
            ch(v.tensor_scalar(gm4[:, :], gs4[:, :], rmx[:, 1:2], None,
                               OP.is_ge))
            ch(v.tensor_tensor(msk8[:, 0:8:2], sc8[:, 0:8:2], gm4[:, :],
                               OP.mult))
            ch(v.tensor_tensor(msk8[:, 1:8:2], sc8[:, 1:8:2], gm4[:, :],
                               OP.mult))
            ch(v.tensor_reduce(rmx[:, 2:3], msk8[:, :], AX, OP.max))
            ch(v.tensor_scalar(t8[:, :], msk8[:, :], rmx[:, 2:3], None,
                               OP.is_equal))
            ch(v.scalar_tensor_tensor(rk8[:, :], t8[:, :], -1e30,
                                      msk8[:, :], OP.mult, OP.add))
            ch(v.tensor_reduce(rmx[:, 3:4], rk8[:, :], AX, OP.max))
            ch(v.tensor_scalar(sel8[:, :], msk8[:, :], rmx[:, 3:4], None,
                               OP.is_ge))
            v.tensor_tensor(wmat[:, 8 * tc:8 * (tc + 1)], msk8[:, :],
                            sel8[:, :], OP.mult).then_inc(c_rt.sem, 1)
        em("vector", f_vrt)

    S3C_LNT = c_lnT.v
    S3C_TP = c_tp.v

    d_tix = C("d_tix")
    d_init = C("d_init")
    d_scat = C("d_scat")
    d_idxl = C("d_idxl")
    d_racc = C("d_racc")
    d_qx = C("d_qx")
    c_qx = C("c_qx")
    d_gq = C("d_gq")
    d_gcp = [C("d_gcp0"), C("d_gcp1")]
    c_selv = C("c_selv")
    c_t1mm = C("c_t1mm")
    c_t1ev = C("c_t1ev")
    c_icv = C("c_icv")
    c_epA = C("c_epA")

    def wsrc(e):
        if e < NEXP:
            return (rwgT[e], rwuT[e], rwdT[e])
        off = (e - NEXP) * F
        return (swgT[:, off:off + F], swuT[:, off:off + F],
                swdT[off:off + F, :])

    h2selT3 = [h2selT[0].rearrange("p (k r) -> p k r", k=KD),
               h2selT[1].rearrange("p (k r) -> p k r", k=KD)]
    RT16 = c_rt.v

    def emit_gtix():
        d_tix.bump(48)

        def f_gtix(g):
            g.wait_ge(c_rt.sem, RT16)
            g.dma_start(out=tixsb[:, :], in_=tixh).then_inc(d_tix.sem, 16)
            g.dma_start(out=lt128sb[:, :],
                        in_=lt128).then_inc(d_tix.sem, 16)
            g.dma_start(out=cidx16[:, :], in_=cidx).then_inc(d_tix.sem, 16)
        em("gpsimd", f_gtix)

    def emit_vrs8():
        vb = c_vch.v
        c_vch.bump(16)
        c_selv.bump()

        def f_vrs8(v, vb=vb):
            pos = [vb]

            def ch(inst):
                pos[0] += 1
                inst.then_inc(c_vch.sem, 1)
                v.wait_ge(c_vch.sem, pos[0])
            v.wait_ge(c_rt.sem, RT16)
            for e8 in range(NEXP):
                ch(v.tensor_single_scalar(pfxB[:, :], wmat[:, e8:128:8],
                                          0.0, OP.is_gt))
                ch(v.tensor_reduce(rs8[:, e8:e8 + 1], pfxB[:, :], AX,
                                   OP.add))
            v.tensor_copy(rs8bf[:, :], rs8[:, :]).then_inc(c_selv.sem, 1)
        em("vector", f_vrs8)

    def emit_tt1():
        c_t1mm.bump()

        def f_tt1(t):
            t.wait_ge(c_selv.sem, 1)
            t.wait_ge(d_tix.sem, 48)
            t.wait_ge(c_rxp.sem, 16)
            t.matmul(ps_b[:, 0:8], lt128sb[:, :], rs8bf[:, :],
                     start=True, stop=True).then_inc(c_t1mm.sem, 1)
        em("tensor", f_tt1)

    def emit_vt1e():
        c_t1ev.bump()

        def f_vt1e(v):
            v.wait_ge(c_t1mm.sem, 1)
            v.tensor_copy(t1all[:, :], ps_b[:, 0:8]).then_inc(c_t1ev.sem, 1)
        em("vector", f_vt1e)

    def emit_vidx(e8):
        vb = c_vch.v
        c_vch.bump(17)
        c_selv.bump()

        def f_vidx(v, e8=e8, vb=vb):
            pos = [vb]

            def ch(inst):
                pos[0] += 1
                inst.then_inc(c_vch.sem, 1)
                v.wait_ge(c_vch.sem, pos[0])
            v.wait_ge(c_t1ev.sem, 1)
            v.wait_ge(d_tix.sem, 48)
            if e8 >= 1:
                v.wait_ge(d_scat.sem, 256 * e8)
            ch(v.tensor_single_scalar(pfxB[:, :], wmat[:, e8:128:8], 0.0,
                                      OP.is_gt))
            ch(v.tensor_copy(pfxA[:, 1:16], pfxB[:, 0:15]))
            ch(v.memset(pfxA[:, 0:1], 0.0))
            cur, oth = pfxA, insd
            for sft in (1, 2, 4, 8):
                ch(v.tensor_tensor(oth[:, sft:16], cur[:, sft:16],
                                   cur[:, 0:16 - sft], OP.add))
                ch(v.tensor_copy(oth[:, 0:sft], cur[:, 0:sft]))
                cur, oth = oth, cur
            ch(v.tensor_scalar(insd[:, :], pfxA[:, :], t1all[:, e8:e8 + 1],
                               None, OP.add))
            ch(v.scalar_tensor_tensor(pfxA[:, :], insd[:, :], -float(CAP),
                                      tixsb[:, :], OP.add, OP.subtract))
            ch(v.tensor_tensor(insd[:, :], pfxA[:, :], pfxB[:, :],
                               OP.mult))
            ch(v.scalar_tensor_tensor(offs[:, :], insd[:, :], float(CAP),
                                      tixsb[:, :], OP.add, OP.add))
            ch(v.tensor_copy(offi[:, :], offs[:, :]))
            ch(v.tensor_copy(pay[:, 0:32:2], tixsb[:, :]))
            v.tensor_copy(pay[:, 1:32:2],
                          wmat[:, e8:128:8]).then_inc(c_selv.sem, 1)
        em("vector", f_vidx)

    def emit_gscat(e8):
        d_init.bump(16)
        d_scat.bump(256)

        def f_gscat(g, e8=e8):
            g.wait_ge(c_rt.sem, RT16)
            g.dma_start(out=idxw[e8][0:CAP, :].rearrange(
                "(g p) w -> p g w", p=128),
                in_=z16[:, 0:2 * NCC].rearrange("p (g w) -> p g w", w=2)
                ).then_inc(d_init.sem, 16)
            g.wait_ge(d_init.sem, 16 * (e8 + 1))
            g.wait_ge(c_selv.sem, e8 + 2)
            for cc8 in range(16):
                g.indirect_dma_start(
                    out=idxw[e8],
                    out_offset=bass.IndirectOffsetOnAxis(
                        offi[:, cc8:cc8 + 1], 0),
                    in_=pay[:, 2 * cc8:2 * cc8 + 2],
                    in_offset=None).then_inc(d_scat.sem, 16)
        em("gpsimd", f_gscat)

    ELIST = [8, 9, 0, 1, 2, 3, 4, 5, 6, 7]
    NEL = len(ELIST) if STAGE == "full" else 0
    smt = [0]
    accslot = [0, 0]
    dmm_base = {}
    epA_base = {}
    SH_ACC = [(0, 0)]

    def emit_wgu_slice(pi, mf, gsrc, usrc):
        sl = FCH * pi + mf
        d_wg[sl % 2].bump(16)
        d_wu[sl % 2].bump(16)

        def f_gwg(g, pi=pi, mf=mf, sl=sl, gsrc=gsrc, usrc=usrc):
            if sl >= 2:
                g.wait_ge(c_gmm.sem, sl - 1)
                g.wait_ge(c_umm.sem, sl - 1)
            if pi == 0 and mf == 0:
                g.wait_ge(c_omm.sem, S3B_OMM)
            g.dma_start(
                out=wgm[:, (sl % 2) * 2048:(sl % 2 + 1) * 2048].rearrange(
                    "p (k c) -> p k c", k=KD),
                in_=gsrc[:, 128 * mf:128 * (mf + 1)].rearrange(
                    "(k p) c -> p k c", p=128)).then_inc(d_wg[sl % 2].sem, 16)
            g.dma_start(
                out=wum[:, (sl % 2) * 2048:(sl % 2 + 1) * 2048].rearrange(
                    "p (k c) -> p k c", k=KD),
                in_=usrc[:, 128 * mf:128 * (mf + 1)].rearrange(
                    "(k p) c -> p k c", p=128)).then_inc(d_wu[sl % 2].sem, 16)
        em("gpsimd", f_gwg)

    for pi in range(NEL):
        xe = ELIST[pi]
        shared = xe >= NEXP
        rp = pi - 2
        Ct = T if shared else CAP
        NMT = Ct // 128
        NTILES = ([(0, 512), (512, 512), (1024, 512), (1536, 512)]
                  if shared else [(0, 512), (512, CAP - 512)])
        par_e = xe % 2
        gsrc, usrc, dsrc = wsrc(xe)
        gudst = gusb if shared else gusel[par_e]
        src3 = h2nT3 if shared else h2selT3[par_e]
        nxt = ELIST[pi + 1] if pi + 1 < NEL else None
        have_nxt = nxt is not None and nxt < NEXP
        nrp = nxt if have_nxt else None
        npar = nxt % 2 if have_nxt else None
        if pi == 2:
            SH_ACC[0] = (d_acc[0].v, d_acc[1].v)

        if pi == 0:
            emit_wgu_slice(pi, 0, gsrc, usrc)
            emit_wgu_slice(pi, 1, gsrc, usrc)
            emit_gtix()
        if have_nxt:
            d_idxl.bump(16)

            def f_gidxl(g, nxt=nxt, nrp=nrp):
                g.wait_ge(d_scat.sem, 256 * (nxt + 1))
                if nrp >= 2:
                    g.wait_ge(c_icv.sem, nrp - 1)
                g.dma_start(
                    out=idxld[nrp % 2][:, 0:2 * NCC].rearrange(
                        "p (g w) -> p g w", w=2),
                    in_=idxw[nxt][0:CAP, :].rearrange(
                        "(g p) w -> p g w", p=128)).then_inc(d_idxl.sem, 16)
            em("gpsimd", f_gidxl)

            d_qx.bump(128)

            def f_gqx(g, nxt=nxt, nrp=nrp):
                g.wait_ge(d_scat.sem, 256 * (nxt + 1))
                if nrp >= 1:
                    g.wait_ge(c_qx.sem, nrp)
                for r in range(8):
                    g.dma_start(
                        out=qidxf[16 * r:16 * (r + 1), 0:2 * 8 * NCC
                                  ].rearrange("q (c w) -> q c w", w=2),
                        in_=idxw[nxt][0:CAP, :].rearrange(
                            "(c q) w -> q c w", q=16)
                        ).then_inc(d_qx.sem, 16)
            em("gpsimd", f_gqx)

            gmm_snap = max(2 * FCH, FCH * pi)
            umm_snap = max(2 * FCH, FCH * pi)

            def emit_gcpair(c8, nrp=nrp, npar=npar, gmn=gmm_snap,
                            umn=umm_snap):
                d_gq.bump(16)

                def f_gq(g, nrp=nrp, npar=npar, c8=c8):
                    g.wait_ge(d_gq.sem, 16 * (NCC * nrp + c8))
                    g.wait_ge(c_qx.sem, nrp + 1)
                    g.wait_ge(d_h2n[0].sem, 128)
                    g.wait_ge(d_h2n[1].sem, 128)
                    cpn = (NCC // 2) * nrp + c8 // 2
                    if cpn:
                        g.wait_ge(d_gcp[c8 % 2].sem, 16 * cpn)
                    g.dma_gather(
                        gstg[c8 % 2][:, :].rearrange("p (k r) -> p k r",
                                                     k=KD),
                        h2nd, qidx16[npar][:, 8 * c8:8 * (c8 + 1)],
                        128, q7reg["n128"], D,
                        transpose=True).then_inc(d_gq.sem, 16)
                em("gpsimd", f_gq)

                d_gcp[c8 % 2].bump(16)

                def f_sycp(sy, nrp=nrp, npar=npar, c8=c8, gmn=gmn,
                           umn=umn):
                    sy.wait_ge(d_gq.sem, 16 * (NCC * nrp + c8 + 1))
                    sy.wait_ge(c_gmm.sem, gmn)
                    sy.wait_ge(c_umm.sem, umn)
                    sy.dma_start(
                        out=h2selT3[npar][:, :, 128 * c8:128 * (c8 + 1)],
                        in_=gstg[c8 % 2][:, :].rearrange(
                            "p (k r) -> p k r", k=KD)
                        ).then_inc(d_gcp[c8 % 2].sem, 16)
                em("sync", f_sycp)
        def emit_wd(k, pi=pi, dsrc=dsrc):
            d_wd.bump(16)

            def f_gwd(g, pi=pi, k=k, dsrc=dsrc,
                      need=dmm_base.get(pi - 1, 0)):
                if pi == 0 and k == 0:
                    g.wait_ge(c_tp.sem, S3C_TP)
                    g.wait_ge(d_o[0].sem, S3B_DO[0])
                    g.wait_ge(d_o[1].sem, S3B_DO[1])
                    g.wait_ge(d_ccin[0].sem, 128)
                    g.wait_ge(d_ccin[1].sem, 128)
                    g.wait_ge(d_h2n[0].sem, 128)
                    g.wait_ge(d_h2n[1].sem, 128)
                    for qq in range(4):
                        g.wait_ge(d_q[qq].sem, S1_DQ[qq])
                if pi >= 1:
                    g.wait_ge(c_dmm.sem, need)
                g.dma_start(out=wdt[:, k * D:(k + 1) * D],
                            in_=dsrc[128 * k:128 * (k + 1), :]
                            ).then_inc(d_wd.sem, 16)
            em("gpsimd", f_gwd)

        for k in range(FCH):
            emit_wd(k)
        WD_NEED = d_wd.v
        for mf in range(2, FCH):
            emit_wgu_slice(pi, mf, gsrc, usrc)
            if pi == 0 and mf - 2 <= 7:
                emit_gscat(mf - 2)
            if have_nxt and pi >= 2 and mf - 2 < NCC:
                emit_gcpair(mf - 2)
        if have_nxt and pi == 1:
            for c8 in range(NCC):
                emit_gcpair(c8)
        if pi + 1 < NEL:
            nx_g, nx_u, _ = wsrc(ELIST[pi + 1])
            emit_wgu_slice(pi + 1, 0, nx_g, nx_u)
            emit_wgu_slice(pi + 1, 1, nx_g, nx_u)
        if pi == 0:
            emit_vrs8()
            emit_tt1()
            emit_vt1e()

        for mf in range(FCH):
            sl = FCH * pi + mf
            c_gmm.bump()

            def f_tg(t, pi=pi, mf=mf, sl=sl, shared=shared, NT=NTILES,
                     src3=src3, epn=epA_base.get(pi - 1, 0), rp=rp):
                t.wait_ge(d_wg[sl % 2].sem, 16 * (sl // 2 + 1))
                if pi == 0 and mf == 0:
                    t.wait_ge(c_lnT.sem, S3C_LNT)
                    t.wait_ge(c_sct.sem, 1)
                if sl >= 1:
                    t.wait_ge(c_gev.sem, sl)
                if mf == 0 and shared and pi >= 1:
                    t.wait_ge(c_epA.sem, epn)
                if mf == 0 and not shared:
                    cp = 16 * (NCC // 2) * (rp + 1)
                    t.wait_ge(d_gcp[0].sem, cp)
                    t.wait_ge(d_gcp[1].sem, cp)
                for k in range(KD):
                    for n0, nw in NT:
                        mm = t.matmul(
                            ps_a[:, n0:n0 + nw],
                            wgm[:, (sl % 2) * 2048 + 128 * k:
                                (sl % 2) * 2048 + 128 * (k + 1)],
                            src3[:, k, n0:n0 + nw],
                            start=(k == 0), stop=(k == KD - 1))
                mm.then_inc(c_gmm.sem, 1)
            em("tensor", f_tg)

            c_umm.bump()

            def f_tu(t, pi=pi, mf=mf, sl=sl, shared=shared, NT=NTILES,
                     src3=src3, epn=epA_base.get(pi - 1, 0)):
                t.wait_ge(d_wu[sl % 2].sem, 16 * (sl // 2 + 1))
                if pi == 0 and mf == 0:
                    t.wait_ge(c_rxp.sem, 16)
                    t.wait_ge(c_t1ev.sem, 1)
                if sl >= 1:
                    t.wait_ge(c_uev.sem, sl)
                if mf == 0 and shared and pi >= 1:
                    t.wait_ge(c_epA.sem, epn)
                for k in range(KD):
                    for n0, nw in NT:
                        mm = t.matmul(
                            ps_b[:, n0:n0 + nw],
                            wum[:, (sl % 2) * 2048 + 128 * k:
                                (sl % 2) * 2048 + 128 * (k + 1)],
                            src3[:, k, n0:n0 + nw],
                            start=(k == 0), stop=(k == KD - 1))
                mm.then_inc(c_umm.sem, 1)
            em("tensor", f_tu)

            c_gev.bump()

            def f_sg(s, sl=sl, Ct=Ct):
                s.wait_ge(c_gmm.sem, sl + 1)
                if sl >= 2:
                    s.wait_ge(c_uev.sem, sl - 1)
                s.activation(gsil[:, (sl % 2) * T:(sl % 2) * T + Ct],
                             ps_a[:, 0:Ct], AF.Silu).then_inc(c_gev.sem, 1)
            em("scalar", f_sg)

            c_uev.bump()
            if shared or pi == 2:
                guneed = dmm_base.get(pi - 1, 0)
            else:
                guneed = dmm_base.get(pi - 2, 0)

            def f_vu(v, mf=mf, sl=sl, Ct=Ct, gudst=gudst, need=guneed,
                     pi=pi):
                v.wait_ge(c_gev.sem, sl + 1)
                v.wait_ge(c_umm.sem, sl + 1)
                if mf == 0 and pi >= 1:
                    v.wait_ge(c_dmm.sem, need)
                v.tensor_tensor(gudst[:, mf * Ct:(mf + 1) * Ct],
                                gsil[:, (sl % 2) * T:(sl % 2) * T + Ct],
                                ps_b[:, 0:Ct], OP.mult).then_inc(c_uev.sem, 1)
            em("vector", f_vu)
            if pi == 0 and mf <= 7:
                emit_vidx(mf)
            if have_nxt and mf == 0:
                c_icv.bump()

                def f_vicv(v, nxt=nxt, nrp=nrp, npar=npar,
                           epn=epA_base.get(pi - 1, 0)):
                    v.wait_ge(d_idxl.sem, 16 * (nrp + 1))
                    if nrp >= 2:
                        v.wait_ge(c_epA.sem, epn)
                    v.tensor_copy(wsel[npar][:, 0:NCC],
                                  idxld[nrp % 2][:, 1:2 * NCC:2]).then_inc(
                                      c_icv.sem, 1)
                em("vector", f_vicv)

                c_qx.bump()

                def f_vqx(v, nrp=nrp, npar=npar):
                    v.wait_ge(d_qx.sem, 128 * (nrp + 1))
                    if nrp >= 2:
                        v.wait_ge(d_racc.sem, 16 * NCC * (nrp - 1))
                        v.wait_ge(d_gq.sem, 16 * NCC * (nrp - 1))
                    v.tensor_copy(qidx16[npar][:, 0:8 * NCC],
                                  qidxf[:, 0:16 * NCC:2]).then_inc(
                                      c_qx.sem, 1)
                em("vector", f_vqx)

        for mt in range(NMT):
            ra = None if shared else NCC * rp + mt
            for np_ in range(2):
                s_idx = smt[0]
                c_dmm.bump()

                def f_td(t, mt=mt, np_=np_, Ct=Ct, gudst=gudst,
                         s_idx=s_idx, wdn=WD_NEED, pi=pi,
                         first=(mt == 0 and np_ == 0)):
                    if first:
                        t.wait_ge(c_uev.sem, FCH * (pi + 1))
                        t.wait_ge(d_wd.sem, wdn)
                        t.wait_ge(c_gev.sem, FCH * (pi + 1))
                    if s_idx >= 2:
                        t.wait_ge(c_epA.sem, s_idx - 1)
                    ps_d = ps_a if np_ == 0 else ps_b
                    for k in range(FCH):
                        for h in range(2):
                            nt2 = 2 * np_ + h
                            mm = t.matmul(
                                ps_d[:, 1024 + h * 512:1024 + (h + 1) * 512],
                                gudst[:, k * Ct + 128 * mt:
                                      k * Ct + 128 * (mt + 1)],
                                wdt[:, k * D + nt2 * 512:
                                    k * D + (nt2 + 1) * 512],
                                start=(k == 0), stop=(k == FCH - 1))
                    mm.then_inc(c_dmm.sem, 1)
                em("tensor", f_td)

                c_epA.bump()

                def f_sep(s, pi=pi, mt=mt, np_=np_, shared=shared,
                          par_e=par_e, dv=c_dmm.v, ra=ra,
                          accn=accslot[mt % 2], icvn=rp + 1,
                          sha=SH_ACC[0],
                          first=(pi == 0 and mt == 0 and np_ == 0)):
                    s.wait_ge(c_dmm.sem, dv)
                    if first:
                        s.wait_ge(d_o[0].sem, S3B_DO[0])
                        s.wait_ge(d_o[1].sem, S3B_DO[1])
                    if np_ == 0:
                        if shared:
                            if accn >= 1:
                                s.wait_ge(d_acc[mt % 2].sem, 16 * accn)
                        else:
                            s.wait_ge(c_icv.sem, icvn)
                            if ra <= 1:
                                s.wait_ge(d_acc[mt % 2].sem, sha[mt % 2])
                            else:
                                s.wait_ge(d_racc.sem, 16 * (ra - 1))
                    ps_d = ps_a if np_ == 0 else ps_b
                    dst = eptmp[:, (mt % 2) * D + np_ * 1024:
                                (mt % 2) * D + (np_ + 1) * 1024]
                    if shared:
                        s.activation(dst, ps_d[:, 1024:2048],
                                     AF.Copy).then_inc(c_epA.sem, 1)
                    else:
                        s.activation(dst, ps_d[:, 1024:2048], AF.Copy,
                                     scale=wsel[par_e][:, mt:mt + 1]
                                     ).then_inc(c_epA.sem, 1)
                em("scalar", f_sep)
                smt[0] += 1

            if shared:
                accslot[mt % 2] += 1
                d_acc[mt % 2].bump(16)
            else:
                d_racc.bump(16)

            def f_gacc(g, pi=pi, mt=mt, shared=shared, par_e=par_e, ra=ra,
                       ev=c_epA.v, sha=SH_ACC[0],
                       first=(pi == 0 and mt == 0)):
                g.wait_ge(c_epA.sem, ev)
                src3d = eptmp[:, (mt % 2) * D:(mt % 2 + 1) * D].rearrange(
                    "p (o r) -> p o r", o=1)
                if shared:
                    if first:
                        g.wait_ge(d_o[0].sem, S3B_DO[0])
                        g.wait_ge(d_o[1].sem, S3B_DO[1])
                        g.wait_ge(d_tix.sem, 48)
                    g.dma_scatter_add(out, src3d,
                                      cidx16[:, 8 * mt:8 * (mt + 1)],
                                      128, q7reg["n128"], D).then_inc(
                                          d_acc[mt % 2].sem, 16)
                else:
                    if ra == 0:
                        g.wait_ge(d_acc[0].sem, sha[0])
                        g.wait_ge(d_acc[1].sem, sha[1])
                    else:
                        g.wait_ge(d_racc.sem, 16 * ra)
                    g.dma_scatter_add(out, src3d,
                                      qidx16[par_e][:, 8 * mt:8 * (mt + 1)],
                                      128, q7reg["n128"],
                                      D).then_inc(d_racc.sem, 16)
            em("gpsimd", f_gacc)
        dmm_base[pi] = c_dmm.v
        epA_base[pi] = c_epA.v

    ACC_ALL = [s.v for s in d_acc]
    RACC_ALL = d_racc.v

    def f_gfin(g):
        if ACC_ALL[0]:
            g.wait_ge(d_acc[0].sem, ACC_ALL[0])
            g.wait_ge(d_acc[1].sem, ACC_ALL[1])
            g.wait_ge(d_racc.sem, RACC_ALL)
        else:
            g.wait_ge(c_rt.sem, 16)
    em("gpsimd", f_gfin)

    def f_syfin(sy):
        sy.wait_ge(d_o[0].sem, S3B_DO[0])
        sy.wait_ge(d_o[1].sem, S3B_DO[1])
    em("sync", f_syfin)

    import contextlib
    with contextlib.ExitStack() as stack:
        for ct in ctrs.values():
            ct.sem = stack.enter_context(nc.semaphore(ct.name))
        with nc.Block() as block:
            block.tensor(lambda t: [f(t) for f in sched["tensor"]])
            block.vector(lambda v: [f(v) for f in sched["vector"]])
            block.scalar(lambda s: [f(s) for f in sched["scalar"]])
            block.sync(lambda sy: [f(sy) for f in sched["sync"]])
            block.gpsimd(lambda g: [f(g) for f in sched["gpsimd"]])
    mybir.codegen_inst_isa_subclasses(nc)
    return nc


_prog_cache = {}


def kernel(L0_attn_x, L0_attn_w_qkv, L0_attn_w_o, L0_ffn_gate_weight,
           L0_ffn_gate_w, L0_ffn_up_w, L0_ffn_down_w, L0_ffn_sh_gate_w,
           L0_ffn_sh_up_w, L0_ffn_sh_down_w):
    x = np.ascontiguousarray(np.asarray(L0_attn_x, dtype=np.float32))
    w_qkv = np.asarray(L0_attn_w_qkv, dtype=np.float32)
    w_o = np.ascontiguousarray(np.asarray(L0_attn_w_o, dtype=np.float32))
    gate_weight = np.asarray(L0_ffn_gate_weight, dtype=np.float32)
    gate_w = np.asarray(L0_ffn_gate_w, dtype=np.float32)
    up_w = np.asarray(L0_ffn_up_w, dtype=np.float32)
    down_w = np.asarray(L0_ffn_down_w, dtype=np.float32)
    sh_gate = np.asarray(L0_ffn_sh_gate_w, dtype=np.float32)
    sh_up = np.asarray(L0_ffn_sh_up_w, dtype=np.float32)
    sh_down = np.asarray(L0_ffn_sh_down_w, dtype=np.float32)

    import ml_dtypes
    identb = np.eye(128, dtype=ml_dtypes.bfloat16)
    identf = np.eye(16, dtype=np.float32)
    lt128 = np.triu(np.ones((128, 128), dtype=np.float32),
                    1).astype(ml_dtypes.bfloat16)
    tixh = (128.0 * np.arange(16, dtype=np.float32)[None, :]
            + np.arange(128, dtype=np.float32)[:, None]).copy()
    z16 = np.zeros((128, 16), dtype=np.float32)
    jj = np.arange(128)
    cidx = np.zeros((16, 128), np.int16)
    for mt16 in range(16):
        cidx[jj % 16, 8 * mt16 + jj // 16] = 128 * mt16 + jj
    cidx = np.tile(cidx, (8, 1))
    cvec_h = np.tile(np.array([[EPS, SBIAS, RBIAS, 0.0]], dtype=np.float32),
                     (128, 1))
    wvq = np.ascontiguousarray(w_qkv[:, 4096:6144])
    gwT = np.ascontiguousarray(gate_weight.T)
    rwgT = np.ascontiguousarray(gate_w.transpose(0, 2, 1))
    rwuT = np.ascontiguousarray(up_w.transpose(0, 2, 1))
    rwdT = np.ascontiguousarray(down_w.transpose(0, 2, 1))
    swgT = np.ascontiguousarray(sh_gate.T)
    swuT = np.ascontiguousarray(sh_up.T)
    swdT = np.ascontiguousarray(sh_down.T)

    in_maps = []
    for c in range(8):
        b, half = divmod(c, 2)
        qcols = w_qkv[:, 1024 * half:1024 * (half + 1)]
        kcols = w_qkv[:, 2048 + 1024 * half:2048 + 1024 * (half + 1)]
        wqk_c = np.ascontiguousarray(np.concatenate([qcols, kcols], axis=1))
        hsv = np.empty((128, 2), dtype=np.float32)
        hsv[:, 0] = 1.0 - half
        hsv[:, 1] = float(half)
        in_maps.append({
            "xb": np.ascontiguousarray(x[b]),
            "xown": np.ascontiguousarray(x[b, 2048 * half:2048 * (half + 1)]),
            "wqk": wqk_c, "wv": wvq, "wo": w_o, "gwT": gwT,
            "rwgT": rwgT, "rwuT": rwuT, "rwdT": rwdT,
            "swgT": swgT, "swuT": swuT, "swdT": swdT,
            "identb": identb, "identf": identf, "hsv": hsv, "cvec": cvec_h,
            "lt128": lt128, "tixh": tixh, "z16": z16, "cidx": cidx,
        })

    nc = build_program()
    res = run_bass_kernel_spmd(nc, in_maps, core_ids=list(range(8)))
    outp = np.empty((4, 4096, 2048), dtype=np.float32)
    for c in range(8):
        b, half = divmod(c, 2)
        outp[b, 2048 * half:2048 * (half + 1)] = res.results[c]["out"]
    return outp
